# version 02 (final)
# baseline (speedup 1.0000x reference)
import numpy as np
from contextlib import ExitStack

import concourse.bass as bass
import concourse.mybir as mybir
import concourse.bass_utils as bass_utils
from concourse.bass_core_rust import IndirectOffsetOnAxis

F32 = mybir.dt.float32
BF16 = mybir.dt.bfloat16
I32 = mybir.dt.int32
AF = mybir.ActivationFunctionType
ALU = mybir.AluOpType
AX = mybir.AxisListType

B, T, D = 2, 4096, 2048
NT = 8
TOK = NT * 128
NH, HD = 16, 128
MI = 4096
MH, MP, MN = 64, 64, 128
CONV_DIM = MI + 2 * MN
PROJ = MI + CONV_DIM + MH
NCT = CONV_DIM // 128
KT = 4
EPS = 1e-5
SCALE_QK = 1.0 / float(np.sqrt(HD))
EXP_OFF = -8.0
NEG_BIG = -1e9

STAGES = 3
import os as _os
TRUNC = int(_os.environ.get('KTRUNC', '9'))


class Sem:
    def __init__(self, h, step=1):
        self.h, self.step, self.n = h, step, 0

    def inc(self, inst):
        self.n += self.step
        inst.then_inc(self.h, self.step)
        return self.n


class Eng:

    def __init__(self, block, name):
        self.block = block
        self.name = name
        self.floor = {}

    def __call__(self, f):
        getattr(self.block, self.name)(f)

    def w(self, e, sem, val):
        if val is None:
            return
        if self.floor.get(id(sem), -1) >= val:
            return
        e.wait_ge(sem.h, val)
        self.floor[id(sem)] = val


def kcol(m):
    return (m % 4) * TOK + (m // 4) * 128


def bkv(m):
    return (m % 4) * 8 + (m // 4)


def build_nc():
    nc = bass.Bass("TRN2", target_bir_lowering=False, debug=False)

    xin = nc.dram_tensor("xin", [TOK, D], F32, kind="ExternalInput")
    w_qkvT = nc.dram_tensor("w_qkvT", [D, 3 * D], F32, kind="ExternalInput")
    w_gT = nc.dram_tensor("w_gT", [D, NH], F32, kind="ExternalInput")
    w_inT = nc.dram_tensor("w_inT", [D, PROJ], F32, kind="ExternalInput")
    w_outT = nc.dram_tensor("w_outT", [MI, D], F32, kind="ExternalInput")
    convw_i = nc.dram_tensor("convw_r", [128, NCT * KT], F32, kind="ExternalInput")
    convb_i = nc.dram_tensor("convb_r", [128, NCT], F32, kind="ExternalInput")
    dtb_i = nc.dram_tensor("dtb_row", [1, MH], F32, kind="ExternalInput")
    alog_i = nc.dram_tensor("alog_row", [1, MH], F32, kind="ExternalInput")
    dvec_i = nc.dram_tensor("dvec_row", [1, MH], F32, kind="ExternalInput")
    normw_i = nc.dram_tensor("normw_r", [128, MI // 128], F32, kind="ExternalInput")
    ident32_i = nc.dram_tensor("ident32", [128, 128], F32, kind="ExternalInput")
    identbf_i = nc.dram_tensor("identbf", [128, 128], BF16, kind="ExternalInput")
    utones_i = nc.dram_tensor("utones", [128, 128], F32, kind="ExternalInput")
    utmask_i = nc.dram_tensor("utmaskbf", [128, 128], BF16, kind="ExternalInput")
    onesrow_i = nc.dram_tensor("onesrow", [1, 128], F32, kind="ExternalInput")
    attmask_i = nc.dram_tensor("attmask", [128, 32 * 128], BF16, kind="ExternalInput")
    zbias_i = nc.dram_tensor("zbias", [128, 32], F32, kind="ExternalInput")
    wmask_i = nc.dram_tensor("wmask", [128, 32], F32, kind="ExternalInput")
    iwmask_i = nc.dram_tensor("iwmask", [128, 32], F32, kind="ExternalInput")
    hmask_i = nc.dram_tensor("hmask", [128, NT], F32, kind="ExternalInput")
    sidx_i = nc.dram_tensor("sidx", [32, 1], I32, kind="ExternalInput")
    sidx128_i = nc.dram_tensor("sidx128", [128, 32], I32, kind="ExternalInput")
    hidx128_i = nc.dram_tensor("hidx128", [128, NT], I32, kind="ExternalInput")

    xout = nc.dram_tensor("xout", [TOK, D], F32, kind="ExternalOutput")
    kv_cb = nc.dram_tensor("kv_cb", [256, TOK], BF16)
    kv_gt = nc.dram_tensor("kv_gt", [1024, TOK], BF16)
    halo_cb = nc.dram_tensor("halo_cb", [NT, 128 * NCT * 3], BF16)
    halo_gt = nc.dram_tensor("halo_gt", [32, 128 * NCT * 3], BF16)
    s_cbd = nc.dram_tensor("s_cb", [NT * 128, MH * MP], BF16)
    s_gtd = nc.dram_tensor("s_gt", [32 * 128, MH * MP], BF16)
    t_cbd = nc.dram_tensor("t_cb", [NT, MH], F32)
    t_gtd = nc.dram_tensor("t_gt", [32, MH], F32)
    t32_d = nc.dram_tensor("t32_d", [32, MH], F32)
    x_spill = nc.dram_tensor("x_spill", [TOK, D], F32)
    hbc_d = nc.dram_tensor("hbc_d", [NCT * 128, TOK], BF16)
    yfm_d = nc.dram_tensor("yfm_d", [NT * 128, MI], BF16)

    RG = [[0, 1, 2, 3], [4, 5, 6, 7]]

    ES = ExitStack()
    with ES:
        def sbt(name, shape, dtype):
            return ES.enter_context(nc.sbuf_tensor(name, shape, dtype))

        def mksem(name, step=1):
            return Sem(ES.enter_context(nc.semaphore(name)), step)

        hw = mksem("s_hw", 16)
        sw = mksem("s_sw", 16)
        cc = mksem("s_cc", 1)
        pe = mksem("s_pe", 1)
        ac = mksem("s_ac", 1)
        dv = mksem("s_dv", 1)
        po = mksem("s_po", 1)

        x_sb = sbt("x_sb", [128, NT * D], F32)
        xhat = sbt("xhat", [128, 16 * TOK], BF16)
        uni = sbt("uni", [128, NT * MI], BF16)
        ident32 = sbt("c_id32", [128, 128], F32)
        identbf = sbt("c_idbf", [128, 128], BF16)
        utones = sbt("c_ut", [128, 128], F32)
        utmask = sbt("c_utm", [128, 128], BF16)
        onesrow = sbt("c_or", [1, 128], F32)
        attmask = sbt("c_am", [128, 32 * 128], BF16)
        zbias = sbt("c_zb", [128, 32], F32)
        wmask = sbt("c_wm", [128, 32], F32)
        iwmask = sbt("c_iwm", [128, 32], F32)
        hmask = sbt("c_hm", [128, NT], F32)
        convw = sbt("c_cw", [128, NCT * KT], F32)
        convb = sbt("c_cb", [128, NCT], F32)
        normw = sbt("c_nw", [128, MI // 128], F32)
        dtb_r = sbt("c_dtb", [1, MH], F32)
        alog_r = sbt("c_al", [1, MH], F32)
        dvec_r = sbt("c_dv", [1, MH], F32)
        sidx = sbt("c_si", [32, 1], I32)
        sidx128 = sbt("c_si128", [128, 32], I32)
        hidx128 = sbt("c_hi128", [128, NT], I32)
        abc = sbt("g_abc", [128, MH], F32)
        dtbbc = sbt("g_dtb", [128, MH], F32)
        dbc = sbt("g_dbc", [128, MH], F32)

        with nc.Block() as block:
            QV = Eng(block, "vector")
            QS = Eng(block, "scalar")
            QP = Eng(block, "tensor")
            QG = Eng(block, "gpsimd")
            QY = Eng(block, "sync")
            ALL_ENG = [QV, QS, QP, QG, QY]

            def _(e):
                xr = xin.ap().rearrange("(n p) d -> p n d", p=128)
                xdst = x_sb.ap().rearrange("p (n d) -> p n d", n=NT)
                QY.w(e, hw, hw.n)
                hw.inc(e.dma_start(out=xdst, in_=xr))
                for dst, src in [
                    (ident32, ident32_i), (identbf, identbf_i), (utones, utones_i),
                    (utmask, utmask_i), (attmask, attmask_i), (zbias, zbias_i),
                    (wmask, wmask_i), (iwmask, iwmask_i), (hmask, hmask_i),
                    (convw, convw_i), (convb, convb_i), (normw, normw_i),
                    (onesrow, onesrow_i),
                    (dtb_r, dtb_i), (alog_r, alog_i), (dvec_r, dvec_i),
                    (sidx, sidx_i), (sidx128, sidx128_i), (hidx128, hidx128_i),
                ]:
                    QY.w(e, hw, hw.n)
                    hw.inc(e.dma_start(out=dst[:, :], in_=src.ap()))
            QY(_)
            p0_t = hw.n

            with nc.psum_tensor("ps_bc", [128, 512], F32) as ps_bc:
                def _(e):
                    QP.w(e, hw, p0_t)
                    e.matmul(ps_bc[:, 0:MH], onesrow[:, :], alog_r[:, :], start=True, stop=True)
                    e.matmul(ps_bc[:, MH:2 * MH], onesrow[:, :], dtb_r[:, :], start=True, stop=True)
                    pe.inc(e.matmul(ps_bc[:, 2 * MH:3 * MH], onesrow[:, :], dvec_r[:, :], start=True, stop=True))
                QP(_)
                t_bc = pe.n

                def _(e):
                    QS.w(e, pe, t_bc)
                    ac.inc(e.activation(abc[:, :], ps_bc[:, 0:MH], AF.Exp))
                QS(_)
                t_abc = ac.n

                def _(e):
                    QV.w(e, ac, t_abc)
                    dv.inc(e.tensor_scalar_mul(abc[:, :], abc[:, :], -1.0))
                    QV.w(e, pe, t_bc)
                    dv.inc(e.tensor_copy(dtbbc[:, :], ps_bc[:, MH:2 * MH]))
                    dv.inc(e.tensor_copy(dbc[:, :], ps_bc[:, 2 * MH:3 * MH]))
                QV(_)
                consts_t = dv.n

            def emit_ffn(tag, floors):
                FS = ExitStack()
                with FS:
                    def fsb(name, shape, dtype):
                        return FS.enter_context(nc.sbuf_tensor(name + tag, shape, dtype))

                    un2 = uni.ap()[:, 16 * TOK:2 * 16 * TOK].bitcast(F32)
                    xc = un2[:, 0:D]
                    xh32 = un2[:, D:2 * D]
                    sq = un2[:, 2 * D:3 * D]
                    stat = fsb("stat", [128, 8], F32)
                    gexp = fsb("gexp", [128, NH], F32)
                    eqc = fsb("eqc", [128, NT * NH], F32)
                    ivc = fsb("ivc", [128, NT * NH], F32)
                    wg = fsb("wg", [128, 16 * NH], F32)
                    wbuf = [fsb("wbufA", [128, 16 * 256], BF16),
                            fsb("wbufB", [128, 16 * 256], BF16)]
                    kacc = fsb("kacc", [128, NT * HD], BF16)
                    vacc = fsb("vacc", [128, NT * HD], BF16)
                    for Q in (QV, QS, QP, QG, QY):
                        def _(e, Q=Q):
                            for s_, v_ in floors.items():
                                Q.w(e, s_, v_)
                        Q(_)

                    xh_ap = xhat.ap()
                    kown = xh_ap[:, 0:TOK]
                    vown = xh_ap[:, TOK:2 * TOK]
                    kall = xh_ap[:, 2 * TOK:6 * TOK]
                    vtmp = xh_ap[:, 6 * TOK:7 * TOK]
                    vaug = xh_ap[:, 7 * TOK:7 * TOK + 32 * (HD + 1)]
                    psb = uni.ap()[:, 16 * TOK + 2 * 6144:16 * TOK + 2 * 6144 + 2048]
                    lrow = fsb("lrow", [128, 24], F32)

                    qv = uni.ap()[:, 0:16 * TOK].rearrange("p (h t) -> p h t", h=16)

                    PP = ExitStack()
                    with PP:
                        ps_tr = PP.enter_context(nc.psum_tensor("ps_tr" + tag, [128, 1024], F32))
                        ps_g = PP.enter_context(nc.psum_tensor("ps_g" + tag, [128, 512], F32))

                        def _(e):
                            QY.w(e, hw, p0_t)
                            wgr = w_gT.ap().rearrange("(dc p) h -> p dc h", p=128)
                            wgd = wg.ap().rearrange("p (dc h) -> p dc h", dc=16)
                            QY.w(e, hw, hw.n)
                            hw.inc(e.dma_start(out=wgd, in_=wgr))
                        QY(_)
                        wg_t = hw.n

                        psg_rd = {"dv": 0, "ac": 0}
                        NTL = int(_os.environ.get('KTILES', str(NT)))
                        KSUB = int(_os.environ.get('KSUB', '9'))
                        for j in range(NTL):
                            xt = x_sb[:, j * D:(j + 1) * D]

                            def _(e, j=j, xt=xt):
                                QV.w(e, hw, p0_t)
                                for s_, v_ in floors.items():
                                    QV.w(e, s_, v_)
                                dv.inc(e.tensor_scalar(xc[:, :], xt, 1.0, 0.0, ALU.mult,
                                                       ALU.add, accum_out=stat[:, 0:1]))
                                QV.w(e, dv, dv.n)
                                dv.inc(e.tensor_scalar_mul(stat[:, 1:2], stat[:, 0:1], 1.0 / D))
                                QV.w(e, dv, dv.n)
                                dv.inc(e.tensor_scalar_sub(xc[:, :], xt, stat[:, 1:2]))
                            QV(_)
                            tln = dv.n

                            def _(e, tln=tln):
                                QS.w(e, dv, tln)
                                ac.inc(e.activation(sq[:, :], xc[:, :], AF.Square,
                                                    accum_out=stat[:, 2:3]))
                                QS.w(e, ac, ac.n)
                                ac.inc(e.activation(stat[:, 3:4], stat[:, 2:3], AF.Sqrt,
                                                    scale=np.float32(1.0 / D), bias=np.float32(EPS)))
                            QS(_)
                            tsq = ac.n

                            def _(e, tsq=tsq):
                                QV.w(e, ac, tsq)
                                dv.inc(e.reciprocal(stat[:, 4:5], stat[:, 3:4]))
                                QV.w(e, dv, dv.n)
                                dv.inc(e.tensor_scalar_mul(xc[:, :], xc[:, :], stat[:, 4:5]))
                            QV(_)
                            txh = dv.n

                            for dc in range(16 if KSUB >= 2 else 0):
                                par = dc % 2

                                def _(e, dc=dc, par=par, txh=txh):
                                    QP.w(e, dv, txh if dc < 2 else txh + 2 * (dc - 1))
                                    pe.inc(e.transpose(ps_tr[:, par * 512:par * 512 + 128],
                                                       xc[:, dc * 128:dc * 128 + 128],
                                                       ident32[:, :]))
                                QP(_)

                                def _(e, dc=dc, par=par, j=j, t=pe.n):
                                    QV.w(e, pe, t)
                                    src = ps_tr[:, par * 512:par * 512 + 128]
                                    dv.inc(e.tensor_copy(
                                        xhat[:, dc * TOK + j * 128:dc * TOK + j * 128 + 128], src))
                                    dv.inc(e.tensor_copy(xh32[:, dc * 128:dc * 128 + 128], src))
                                QV(_)
                            tr_all = dv.n

                            if KSUB < 3:
                                continue

                            def _(e, tr_all=tr_all):
                                QP.w(e, dv, tr_all)
                                QP.w(e, hw, wg_t)
                                QP.w(e, dv, psg_rd["dv"])
                                QP.w(e, ac, psg_rd["ac"])
                                mm = None
                                for dc in range(16):
                                    mm = e.matmul(ps_g[:, 0:NH], xh32[:, dc * 128:(dc + 1) * 128],
                                                  wg[:, dc * NH:(dc + 1) * NH],
                                                  start=(dc == 0), stop=(dc == 15))
                                pe.inc(mm)
                            QP(_)
                            tg = pe.n

                            if KSUB < 4:
                                continue

                            def _(e, j=j, tg=tg):
                                QV.w(e, pe, tg)
                                dv.inc(e.reduce_max(stat[:, 5:6], ps_g[:, 0:NH], axis=AX.X))
                                QV.w(e, dv, dv.n)
                                dv.inc(e.tensor_single_scalar(eqc[:, j * NH:(j + 1) * NH],
                                                              ps_g[:, 0:NH], stat[:, 5:6],
                                                              ALU.is_equal))
                                dv.inc(e.tensor_scalar_mul(stat[:, 6:7], stat[:, 5:6], -1.0))
                            QV(_)
                            tmx = dv.n

                            def _(e, tmx=tmx):
                                QS.w(e, dv, tmx)
                                ac.inc(e.activation(gexp[:, :], ps_g[:, 0:NH], AF.Exp,
                                                    bias=stat[:, 6:7]))
                            QS(_)
                            tge = ac.n
                            psg_rd["ac"] = tge

                            def _(e, j=j, tge=tge):
                                QV.w(e, ac, tge)
                                dv.inc(e.reduce_sum(stat[:, 7:8], gexp[:, :], axis=AX.X))
                                QV.w(e, dv, dv.n)
                                dv.inc(e.reciprocal(stat[:, 7:8], stat[:, 7:8]))
                                QV.w(e, dv, dv.n)
                                dv.inc(e.tensor_scalar_mul(ivc[:, j * NH:(j + 1) * NH],
                                                           eqc[:, j * NH:(j + 1) * NH],
                                                           stat[:, 7:8]))
                            QV(_)
                            psg_rd["dv"] = dv.n
                        pre_dv = dv.n
                        pre_ac = ac.n
                        if TRUNC < 2:
                            return

                        ld_t = {}
                        mm_done = {}
                        psg_free = dv.n

                        def loadw(e, buf, col0):
                            QG.w(e, sw, sw.n)
                            src = w_qkvT.ap()[:, col0:col0 + 256].rearrange(
                                "(dc p) o -> p dc o", p=128)
                            dstv = buf.ap().rearrange("p (dc o) -> p dc o", dc=16)
                            sw.inc(e.dma_start(out=dstv, in_=src))

                        for oc in range(24):
                            buf = wbuf[oc % 2]

                            def _(e, oc=oc, buf=buf):
                                if oc >= 2:
                                    QG.w(e, pe, mm_done[oc - 2])
                                loadw(e, buf, 256 * oc)
                            QG(_)
                            ld_t[oc] = sw.n

                            if oc < 8:
                                for half in range(2):
                                    h = 2 * oc + half
                                    for tb in range(2):
                                        def _(e, oc=oc, half=half, tb=tb, buf=buf,
                                              pf=psg_free):
                                            QP.w(e, sw, ld_t[oc])
                                            QP.w(e, dv, max(pre_dv, pf))
                                            QP.w(e, ac, pre_ac)
                                            mm = None
                                            for dc in range(16):
                                                mm = e.matmul(
                                                    ps_g[:, 0:512],
                                                    buf[:, dc * 256 + half * 128:dc * 256 + half * 128 + 128],
                                                    xhat[:, dc * TOK + tb * 512:dc * TOK + tb * 512 + 512],
                                                    start=(dc == 0), stop=(dc == 15))
                                            pe.inc(mm)
                                        QP(_)
                                        mm_done[oc] = pe.n

                                        def _(e, h=h, tb=tb, t=pe.n):
                                            QV.w(e, pe, t)
                                            dv.inc(e.tensor_copy(
                                                uni[:, h * TOK + tb * 512:h * TOK + tb * 512 + 512],
                                                ps_g[:, 0:512]))
                                        QV(_)
                                        psg_free = dv.n
                            else:
                                kvi = oc - 8
                                is_k = kvi < 8
                                h0 = (kvi % 8) * 2
                                acc = kacc if is_k else vacc
                                sel = eqc if is_k else ivc
                                for j in range(NT):
                                    def _(e, oc=oc, j=j, buf=buf, pf=psg_free):
                                        QP.w(e, sw, ld_t[oc])
                                        QP.w(e, dv, max(pre_dv, pf))
                                        QP.w(e, ac, pre_ac)
                                        mm = None
                                        for dc in range(16):
                                            mm = e.matmul(
                                                ps_g[:, 0:256],
                                                xhat[:, dc * TOK + j * 128:dc * TOK + j * 128 + 128],
                                                buf[:, dc * 256:(dc + 1) * 256],
                                                start=(dc == 0), stop=(dc == 15))
                                        pe.inc(mm)
                                    QP(_)
                                    mm_done[oc] = pe.n

                                    def _(e, j=j, h0=h0, acc=acc, sel=sel,
                                          first=(kvi == 0 or kvi == 8), t=pe.n):
                                        QV.w(e, pe, t)
                                        for hh in range(2):
                                            h = h0 + hh
                                            dst = acc[:, j * HD:(j + 1) * HD]
                                            col = sel[:, j * NH + h:j * NH + h + 1]
                                            src = ps_g[:, hh * 128:(hh + 1) * 128]
                                            if first and hh == 0:
                                                dv.inc(e.tensor_single_scalar(dst, src, col, ALU.mult))
                                            else:
                                                QV.w(e, dv, dv.n)
                                                dv.inc(e.scalar_tensor_tensor(dst, src, col, dst,
                                                                              ALU.mult, ALU.add))
                                    QV(_)
                                    psg_free = dv.n
                        qkv_dv = dv.n
                        if TRUNC < 3:
                            return

                        with nc.psum_tensor("ps_tb" + tag, [128, 2048], BF16) as ps_tb:
                            tb_evac = {0: qkv_dv, 1: qkv_dv}
                            for idx in range(2 * NT):
                                j, which = idx // 2, idx % 2
                                src = (kacc if which == 0 else vacc)[:, j * HD:(j + 1) * HD]
                                dst = (kown if which == 0 else vown)[:, j * 128:(j + 1) * 128]
                                par = idx % 2

                                def _(e, src=src, par=par, t=tb_evac[par]):
                                    QP.w(e, dv, max(qkv_dv, t))
                                    pe.inc(e.transpose(ps_tb[:, par * 1024:par * 1024 + 128],
                                                       src, identbf[:, :]))
                                QP(_)

                                def _(e, dst=dst, par=par, t=pe.n):
                                    QV.w(e, pe, t)
                                    dv.inc(e.tensor_copy(dst, ps_tb[:, par * 1024:par * 1024 + 128]))
                                QV(_)
                                tb_evac[par] = dv.n
                            ko_t = dv.n

                            def _(e, t=ko_t):
                                QY.w(e, dv, t)
                                QY.w(e, hw, hw.n)
                                hw.inc(e.dma_start(out=kv_cb.ap()[0:128, :], in_=kown[:, :]))
                                QY.w(e, hw, hw.n)
                                hw.inc(e.dma_start(out=kv_cb.ap()[128:256, :], in_=vown[:, :]))
                            QY(_)
                            cb_t = hw.n

                            def _(e, t=cb_t):
                                QG.w(e, hw, t)
                                cc.inc(e.collective_compute(
                                    "AllGather", ALU.bypass, replica_groups=RG,
                                    ins=[kv_cb.ap()], outs=[kv_gt.ap()]))
                            QG(_)
                            cc_t = cc.n

                            def _(e, t=cc_t):
                                QY.w(e, cc, t)
                                for r in range(4):
                                    QY.w(e, hw, hw.n)
                                    hw.inc(e.dma_start(out=kall[:, r * TOK:(r + 1) * TOK],
                                                       in_=kv_gt.ap()[256 * r:256 * r + 128, :]))
                            QY(_)
                            kall_t = hw.n

                            def _(e):
                                QV.w(e, dv, ko_t)
                                ones_cols = vaug.rearrange(
                                    "p (b c) -> p b c", b=32)[:, :, 0:1]
                                dv.inc(e.memset(ones_cols, 1.0))
                            QV(_)
                            vz_t = dv.n

                            vtr_last = {"pe": pe.n}
                            for r in range(4):
                                def _(e, r=r, t=cc_t):
                                    QY.w(e, cc, t)
                                    if r > 0:
                                        QY.w(e, pe, vtr_last["pe"])
                                    QY.w(e, hw, hw.n)
                                    hw.inc(e.dma_start(
                                        out=vtmp[:, :],
                                        in_=kv_gt.ap()[256 * r + 128:256 * r + 256, :]))
                                QY(_)
                                vt_t = hw.n

                                for jj in range(NT):
                                    bk = r * 8 + jj
                                    par = jj % 2

                                    def _(e, jj=jj, par=par, vt_t=vt_t, t=tb_evac[par]):
                                        QP.w(e, hw, vt_t)
                                        QP.w(e, dv, t)
                                        pe.inc(e.transpose(ps_tb[:, par * 1024:par * 1024 + 128],
                                                           vtmp[:, jj * 128:(jj + 1) * 128],
                                                           identbf[:, :]))
                                    QP(_)
                                    vtr_last["pe"] = pe.n

                                    def _(e, bk=bk, par=par, t=pe.n, vz_t=vz_t):
                                        QV.w(e, pe, t)
                                        QV.w(e, dv, vz_t)
                                        dv.inc(e.tensor_copy(
                                            vaug[:, bk * 129 + 1:bk * 129 + 129],
                                            ps_tb[:, par * 1024:par * 1024 + 128]))
                                    QV(_)
                                    tb_evac[par] = dv.n
                            vaug_t = dv.n
                    if TRUNC < 4:
                        return

                    AS = ExitStack()
                    with AS:
                        ps_s = [AS.enter_context(nc.psum_tensor("ps_s%d" % k + tag,
                                                                [128, 1024], F32))
                                for k in range(2)]
                        ps_y = [AS.enter_context(nc.psum_tensor("ps_y%d" % k + tag,
                                                                [128, 512], F32))
                                for k in range(3)]

                        fin_t = dv.n
                        act_par = [ac.n, ac.n]
                        mm2_par = [pe.n, pe.n]
                        po_par = [po.n, po.n]
                        for hg in range(2):
                            for j in range(NT):
                                nm = 4 * j + 4
                                for m in range(nm):
                                    par = m % 2
                                    i = m - 4 * j

                                    def _(e, m=m, par=par, j=j, hg=hg,
                                          war_ac=act_par[par]):
                                        QP.w(e, ac, war_ac)
                                        QP.w(e, hw, kall_t)
                                        QP.w(e, dv, max(vaug_t, fin_t))
                                        for a in range(2):
                                            mm = e.matmul(
                                                ps_s[par][:, a * 512:(a + 1) * 512],
                                                kall[:, kcol(m):kcol(m) + 128],
                                                qv[:, hg * 8 + 4 * a:hg * 8 + 4 * a + 4,
                                                   j * 128:j * 128 + 128],
                                                start=True, stop=True)
                                        pe.inc(mm)
                                    QP(_)
                                    t_mm1 = pe.n

                                    def _(e, par=par, j=j, i=i, t1=t_mm1,
                                          wmm=mm2_par[par], wpo=po_par[par]):
                                        QS.w(e, pe, max(t1, wmm))
                                        QS.w(e, po, wpo)
                                        bias = (zbias[:, 4 * j + i:4 * j + i + 1]
                                                if i >= 0 else np.float32(EXP_OFF))
                                        ac.inc(e.activation(psb[:, par * 1024:par * 1024 + 1024],
                                                            ps_s[par][:, 0:1024], AF.Exp,
                                                            bias=bias, scale=np.float32(SCALE_QK)))
                                    QS(_)
                                    t_act = ac.n
                                    act_par[par] = t_act

                                    t_pool = None
                                    if i >= 0:
                                        def _(e, par=par, j=j, i=i, t=t_act):
                                            QG.w(e, ac, t)
                                            dst = psb[:, par * 1024:par * 1024 + 1024].rearrange("p (h q) -> p h q", h=8)
                                            msk = attmask.ap()[:, (4 * j + i) * 128:(4 * j + i + 1) * 128]
                                            msk = msk.unsqueeze(1).broadcast_to([128, 8, 128])
                                            po.inc(e.tensor_tensor(dst, dst, msk, ALU.mult))
                                        QG(_)
                                        t_pool = po.n
                                        po_par[par] = t_pool

                                    def _(e, m=m, par=par, ta=t_act, tp=t_pool,
                                          first=(m == 0), last=(m == nm - 1)):
                                        QP.w(e, ac, ta)
                                        if tp is not None:
                                            QP.w(e, po, tp)
                                        mm = None
                                        for hh in range(8):
                                            st_ = first and (hh % 3 == 0)
                                            sp_ = last and (hh in (2, 5, 7))
                                            mm = e.matmul(
                                                ps_y[hh // 3][:, (hh % 3) * 129:(hh % 3) * 129 + 129],
                                                psb[:, par * 1024 + hh * 128:par * 1024 + hh * 128 + 128],
                                                vaug[:, bkv(m) * 129:bkv(m) * 129 + 129],
                                                start=st_, stop=sp_)
                                        pe.inc(mm)
                                    QP(_)
                                    mm2_par[par] = pe.n
                                t_last = pe.n

                                def _(e, hg=hg, j=j, t=t_last, fl=fin_t):
                                    QV.w(e, pe, t)
                                    QV.w(e, dv, fl)
                                    for hh in range(8):
                                        dv.inc(e.tensor_copy(
                                            lrow[:, hh:hh + 1],
                                            ps_y[hh // 3][:, (hh % 3) * 129:(hh % 3) * 129 + 1]))
                                    QV.w(e, dv, dv.n)
                                    dv.inc(e.reciprocal(lrow[:, 8:16], lrow[:, 0:8]))
                                    QV.w(e, dv, dv.n)
                                    dv.inc(e.tensor_scalar_mul(lrow[:, 16:24], lrow[:, 8:16], 0.5))
                                    QV.w(e, dv, dv.n)
                                    for hh in range(8):
                                        h = hg * 8 + hh
                                        xs = x_sb[:, j * D + h * 128:j * D + h * 128 + 128]
                                        dv.inc(e.scalar_tensor_tensor(
                                            xs,
                                            ps_y[hh // 3][:, (hh % 3) * 129 + 1:(hh % 3) * 129 + 129],
                                            lrow[:, 16 + hh:17 + hh], xs, ALU.mult, ALU.add))
                                QV(_)
                                fin_t = dv.n
                    return


            def emit_mamba(floors):
                MS = ExitStack()
                with MS:
                    def msb(name, shape, dtype):
                        return MS.enter_context(nc.sbuf_tensor(name + "_m", shape, dtype))

                    for Q in (QV, QS, QP, QG, QY):
                        def _(e, Q=Q):
                            for s_, v_ in floors.items():
                                Q.w(e, s_, v_)
                        Q(_)

                    REG = x_sb.ap()
                    UN2 = uni.ap()[:, 16 * TOK:2 * 16 * TOK].bitcast(F32)
                    t32r = REG[:, 12646:14694][0:1, :]
                    wdt = uni.ap()[:, 16 * TOK + 8192:16 * TOK + 9216]
                    dtraw = UN2[:, 4608:5120]
                    whbc = [uni.ap()[:, 16 * TOK + 10240:16 * TOK + 12288],
                            uni.ap()[:, 16 * TOK + 12288:16 * TOK + 14336]]
                    wgb = [REG[:, 0:2048].bitcast(BF16),
                           REG[:, 2048:4096].bitcast(BF16)]

                    dtv = msb("dtv", [128, 512], F32)
                    dav = msb("dav", [128, 512], F32)
                    sc1 = msb("sc1", [128, 512], F32)
                    sc2 = msb("sc2", [128, 512], F32)
                    acum = msb("acum", [128, 512], F32)
                    nacum = msb("nacum", [128, 512], F32)
                    dte = msb("dte", [128, 512], F32)
                    eav = msb("eav", [128, 512], F32)
                    trow = msb("trow", [1, 512], F32)
                    call = msb("call", [128, NT * 128], BF16)
                    ball = msb("ball", [128, 2 * 128], BF16)
                    btm = msb("btm", [128, 128], BF16)
                    cbm = msb("cbm", [128, 128], BF16)
                    dmt = [msb("dmtA", [128, 128], F32), msb("dmtB", [128, 128], F32)]
                    pex = [msb("pexA", [128, 128], F32), msb("pexB", [128, 128], F32)]
                    ptb = [msb("ptbA", [128, 128], BF16), msb("ptbB", [128, 128], BF16)]
                    halo_cmp = msb("halocmp", [128, NCT * 3], BF16)
                    halo_sb = msb("halosb", [128, NCT * 3], BF16)
                    hgslc = msb("hgslc", [128, 512], BF16)
                    hgslc2 = msb("hgslc2", [128, 512], BF16)
                    xrd = msb("xrd", [128, 512], F32)
                    gsil = msb("gsil", [128, 256], BF16)
                    rstat = msb("rstat", [128, 4], F32)

                    xc2 = UN2[:, 0:D]
                    sq2 = UN2[:, D:2 * D]
                    LNPP = ExitStack()
                    with LNPP:
                        ps_tr2 = LNPP.enter_context(
                            nc.psum_tensor("ps_tr_m", [128, 1024], F32))
                        for j in range(NT):
                            xt = x_sb[:, j * D:(j + 1) * D]

                            def _(e, j=j, xt=xt):
                                for s_, v_ in floors.items():
                                    QV.w(e, s_, v_)
                                dv.inc(e.tensor_scalar(xc2[:, :], xt, 1.0, 0.0, ALU.mult,
                                                       ALU.add, accum_out=rstat[:, 0:1]))
                                QV.w(e, dv, dv.n)
                                dv.inc(e.tensor_scalar_mul(rstat[:, 1:2], rstat[:, 0:1], 1.0 / D))
                                QV.w(e, dv, dv.n)
                                dv.inc(e.tensor_scalar_sub(xc2[:, :], xt, rstat[:, 1:2]))
                            QV(_)
                            tln = dv.n

                            def _(e, tln=tln):
                                QS.w(e, dv, tln)
                                ac.inc(e.activation(sq2[:, :], xc2[:, :], AF.Square,
                                                    accum_out=rstat[:, 2:3]))
                                QS.w(e, ac, ac.n)
                                ac.inc(e.activation(rstat[:, 3:4], rstat[:, 2:3], AF.Sqrt,
                                                    scale=np.float32(1.0 / D),
                                                    bias=np.float32(EPS)))
                            QS(_)
                            tsq = ac.n

                            def _(e, tsq=tsq):
                                QV.w(e, ac, tsq)
                                dv.inc(e.reciprocal(rstat[:, 2:3], rstat[:, 3:4]))
                                QV.w(e, dv, dv.n)
                                dv.inc(e.tensor_scalar_mul(xc2[:, :], xc2[:, :], rstat[:, 2:3]))
                            QV(_)
                            txh = dv.n

                            for dc in range(16):
                                par = dc % 2

                                def _(e, dc=dc, par=par, txh=txh):
                                    QP.w(e, dv, txh if dc < 2 else txh + (dc - 1))
                                    pe.inc(e.transpose(ps_tr2[:, par * 512:par * 512 + 128],
                                                       xc2[:, dc * 128:dc * 128 + 128],
                                                       ident32[:, :]))
                                QP(_)

                                def _(e, dc=dc, par=par, j=j, t=pe.n):
                                    QV.w(e, pe, t)
                                    dv.inc(e.tensor_copy(
                                        xhat[:, dc * TOK + j * 128:dc * TOK + j * 128 + 128],
                                        ps_tr2[:, par * 512:par * 512 + 128]))
                                QV(_)
                        ln2_dv = dv.n

                    def _(e):
                        QY.w(e, dv, ln2_dv)
                        xo = x_spill.ap().rearrange("(n p) d -> p n d", p=128)
                        xsrc = x_sb.ap().rearrange("p (n d) -> p n d", n=NT)
                        QY.w(e, hw, hw.n)
                        hw.inc(e.dma_start(out=xo, in_=xsrc))
                    QY(_)
                    spill_t = hw.n

                    MA = ExitStack()
                    with MA:
                        ps_dt = MA.enter_context(nc.psum_tensor("ps_dt_m", [128, 512], F32))
                        ps_t2 = MA.enter_context(nc.psum_tensor("ps_t2_m", [128, 512], F32))

                        def _(e):
                            QG.w(e, hw, p0_t)
                            QG.w(e, sw, sw.n)
                            src = w_inT.ap()[:, PROJ - MH:PROJ].rearrange(
                                "(dc p) h -> p dc h", p=128)
                            dst = wdt.rearrange("p (dc h) -> p dc h", dc=16)
                            sw.inc(e.dma_start(out=dst, in_=src))
                        QG(_)
                        wdt_t = sw.n

                        dt_ev = dv.n
                        for j in range(NT):
                            def _(e, j=j, ev=dt_ev):
                                QP.w(e, sw, wdt_t)
                                QP.w(e, dv, max(ln2_dv, ev))
                                mm = None
                                for dc in range(16):
                                    mm = e.matmul(ps_dt[:, 0:MH],
                                                  xhat[:, dc * TOK + j * 128:dc * TOK + j * 128 + 128],
                                                  wdt[:, dc * MH:(dc + 1) * MH],
                                                  start=(dc == 0), stop=(dc == 15))
                                pe.inc(mm)
                            QP(_)
                            tmm = pe.n

                            def _(e, j=j, tmm=tmm):
                                QV.w(e, pe, tmm)
                                dv.inc(e.tensor_copy(dtraw[:, j * MH:(j + 1) * MH], ps_dt[:, 0:MH]))
                            QV(_)
                            dt_ev = dv.n
                        dtr_t = dv.n

                        def _(e):
                            QV.w(e, dv, dtr_t)
                            dbb = dtbbc.ap().unsqueeze(1).broadcast_to([128, NT, MH])
                            dv.inc(e.tensor_tensor(
                                sc1.ap().rearrange("p (n h) -> p n h", n=NT),
                                dtraw.rearrange("p (n h) -> p n h", n=NT),
                                dbb, ALU.add))
                            QV.w(e, dv, dv.n)
                            dv.inc(e.tensor_relu(dtv[:, :], sc1[:, :]))
                        QV(_)
                        t1 = dv.n

                        def _(e, t1=t1):
                            QS.w(e, dv, t1)
                            ac.inc(e.activation(sc2[:, :], sc1[:, :], AF.Abs))
                            QS.w(e, ac, ac.n)
                            ac.inc(e.activation(sc2[:, :], sc2[:, :], AF.Exp,
                                                scale=np.float32(-1.0)))
                            QS.w(e, ac, ac.n)
                            ac.inc(e.activation(sc2[:, :], sc2[:, :], AF.Ln,
                                                bias=np.float32(1.0)))
                        QS(_)
                        t2 = ac.n

                        def _(e, t2=t2):
                            QV.w(e, ac, t2)
                            dv.inc(e.tensor_add(dtv[:, :], dtv[:, :], sc2[:, :]))
                            QV.w(e, dv, dv.n)
                            abb = abc.ap().unsqueeze(1).broadcast_to([128, NT, MH])
                            dv.inc(e.tensor_tensor(
                                dav.ap().rearrange("p (n h) -> p n h", n=NT),
                                dtv.ap().rearrange("p (n h) -> p n h", n=NT),
                                abb, ALU.mult))
                        QV(_)
                        dav_t = dv.n

                        ev_pa = dv.n
                        for j in range(NT):
                            def _(e, j=j, ev=ev_pa):
                                QP.w(e, dv, max(dav_t, ev))
                                e.matmul(ps_t2[:, 0:MH], utones[:, :],
                                         dav[:, j * MH:(j + 1) * MH], start=True, stop=True)
                                pe.inc(e.matmul(ps_t2[0:1, MH:2 * MH], utones[:, 127:128],
                                                dav[:, j * MH:(j + 1) * MH],
                                                start=True, stop=True))
                            QP(_)
                            tac = pe.n

                            def _(e, j=j, tac=tac):
                                QV.w(e, pe, tac)
                                dv.inc(e.tensor_copy(acum[:, j * MH:(j + 1) * MH], ps_t2[:, 0:MH]))
                                dv.inc(e.tensor_copy(trow[0:1, j * MH:(j + 1) * MH],
                                                     ps_t2[0:1, MH:2 * MH]))
                                QV.w(e, dv, dv.n)
                                dv.inc(e.tensor_scalar_mul(nacum[:, j * MH:(j + 1) * MH],
                                                           acum[:, j * MH:(j + 1) * MH], -1.0))
                            QV(_)
                            tna = dv.n

                            def _(e, j=j, t=tna):
                                QP.w(e, dv, t)
                                pe.inc(e.matmul(ps_dt[:, 128:128 + MH], onesrow[:, :],
                                                trow[0:1, j * MH:(j + 1) * MH],
                                                start=True, stop=True))
                            QP(_)
                            tbc = pe.n

                            def _(e, j=j, tbc=tbc):
                                QV.w(e, pe, tbc)
                                dv.inc(e.tensor_tensor(sc1[:, 0:MH], ps_dt[:, 128:128 + MH],
                                                       nacum[:, j * MH:(j + 1) * MH], ALU.add))
                            QV(_)
                            trel = dv.n

                            def _(e, trel=trel):
                                QS.w(e, dv, trel)
                                ac.inc(e.activation(sc2[:, 0:MH], sc1[:, 0:MH], AF.Exp))
                            QS(_)
                            texp = ac.n

                            def _(e, j=j, texp=texp):
                                QV.w(e, ac, texp)
                                dv.inc(e.tensor_mul(dte[:, j * MH:(j + 1) * MH],
                                                    sc2[:, 0:MH], dtv[:, j * MH:(j + 1) * MH]))
                            QV(_)
                            ev_pa = dv.n
                        acum_dv = dv.n

                        def _(e):
                            QS.w(e, dv, acum_dv)
                            ac.inc(e.activation(eav[:, :], acum[:, :], AF.Exp))
                        QS(_)
                        eav_t = ac.n

                        def _(e):
                            QY.w(e, dv, acum_dv)
                            tdst = t_cbd.ap().rearrange("n h -> (n h)").unsqueeze(0)
                            QY.w(e, hw, hw.n)
                            hw.inc(e.dma_start(out=tdst, in_=trow[0:1, :]))
                        QY(_)
                        tcb_t = hw.n

                    hbc_sp = REG[:, 0:4454].bitcast(BF16)
                    xs_sp = REG[:, 4454:8550].bitcast(BF16)
                    xr_tm = REG[:, 8550:10598].bitcast(BF16)
                    scon = REG[:, 10598:12646].bitcast(BF16)

                    PA = ExitStack()
                    with PA:
                        ps_h = PA.enter_context(nc.psum_tensor("ps_h_m", [128, 512], F32))
                        wld = {}
                        hbc_ev = dv.n
                        sp_store = hw.n
                        for sp in range(4):
                            for ct in range(NCT):
                                buf = whbc[ct % 2]

                                def _(e, sp=sp, ct=ct, buf=buf):
                                    QG.w(e, sw, sw.n)
                                    k2 = sp * NCT + ct - 2
                                    if k2 >= 0:
                                        QG.w(e, pe, wld[k2])
                                    src = w_inT.ap()[:, MI + 128 * ct:MI + 128 * ct + 128] \
                                        .rearrange("(dc p) o -> p dc o", p=128)
                                    dstv = buf.rearrange("p (dc o) -> p dc o", dc=16)
                                    sw.inc(e.dma_start(out=dstv, in_=src))
                                QG(_)
                                ldt = sw.n

                                def _(e, sp=sp, ct=ct, buf=buf, ldt=ldt, ev=hbc_ev):
                                    QP.w(e, sw, ldt)
                                    QP.w(e, dv, max(ln2_dv, ev))
                                    QP.w(e, hw, spill_t)
                                    mm = None
                                    for dc in range(16):
                                        mm = e.matmul(
                                            ps_h[:, 0:256],
                                            buf[:, dc * 128:(dc + 1) * 128],
                                            xhat[:, dc * TOK + sp * 256:dc * TOK + sp * 256 + 256],
                                            start=(dc == 0), stop=(dc == 15))
                                    pe.inc(mm)
                                QP(_)
                                wld[sp * NCT + ct] = pe.n

                                def _(e, sp=sp, ct=ct, t=pe.n, st=sp_store):
                                    QV.w(e, pe, t)
                                    QV.w(e, hw, st)
                                    dv.inc(e.tensor_copy(
                                        hbc_sp[:, ct * 262 + 3:ct * 262 + 131],
                                        ps_h[:, 0:128]))
                                    dv.inc(e.tensor_copy(
                                        hbc_sp[:, ct * 262 + 134:ct * 262 + 262],
                                        ps_h[:, 128:256]))
                                QV(_)
                                hbc_ev = dv.n

                            def _(e, sp=sp, t=hbc_ev):
                                QY.w(e, dv, t)
                                for loc in range(2):
                                    src = hbc_sp.rearrange("p (ct l q) -> p ct l q",
                                                           ct=NCT, l=2)[:, :, loc:loc + 1,
                                                                        3:131].squeeze(2)
                                    tk = (2 * sp + loc) * 128
                                    dst = hbc_d.ap().rearrange("(ct p) T -> p ct T",
                                                               p=128)[:, :, tk:tk + 128]
                                    QY.w(e, hw, hw.n)
                                    hw.inc(e.dma_start(out=dst, in_=src))
                            QY(_)

                            for loc in range(2):
                                jj = 2 * sp + loc

                                def _(e, loc=loc, t=hbc_ev):
                                    QV.w(e, dv, t)
                                    QV.w(e, dv, dv.n)
                                    QV.w(e, hw, hw.n)
                                    src = hbc_sp.rearrange("p (ct q) -> p ct q", ct=NCT)[
                                        :, :, loc * 131 + 128:loc * 131 + 131]
                                    dv.inc(e.tensor_copy(
                                        halo_cmp.ap().rearrange("p (ct k) -> p ct k", ct=NCT),
                                        src))
                                QV(_)

                                def _(e, jj=jj, t=dv.n):
                                    QY.w(e, dv, t)
                                    dst = halo_cb.ap().rearrange("n (p q) -> n p q", p=128)[
                                        jj:jj + 1, :, :]
                                    QY.w(e, hw, hw.n)
                                    hw.inc(e.dma_start(out=dst, in_=halo_cmp[:, :]))
                                QY(_)
                            sp_store = hw.n
                        hbcd_t = hw.n

                        def _(e, t=hbcd_t):
                            QG.w(e, hw, t)
                            cc.inc(e.collective_compute(
                                "AllGather", ALU.bypass, replica_groups=RG,
                                ins=[halo_cb.ap()], outs=[halo_gt.ap()]))
                        QG(_)
                        halo_cc = cc.n

                    PB = ExitStack()
                    with PB:
                        ps_cb = PB.enter_context(nc.psum_tensor("ps_cb_m", [128, 512], F32))
                        ps_at = PB.enter_context(nc.psum_tensor("ps_at_m", [128, 1024], F32))
                        ps_ym = PB.enter_context(nc.psum_tensor("ps_ym_m", [128, 512], F32))
                        ps_st = PB.enter_context(nc.psum_tensor("ps_st_m", [128, 512], F32))
                        ps_tb2 = PB.enter_context(nc.psum_tensor("ps_tb_m", [128, 2048], BF16))

                        conv_rd = dv.n
                        halo_place = dv.n
                        at_par = {0: dv.n, 1: dv.n}
                        tb_par = {0: dv.n, 1: dv.n}
                        ymst_ev = dv.n
                        scon_st = hw.n
                        xr_rd = dv.n
                        for sp in range(int(_os.environ.get('KSPS', '4'))):
                            def _(e, sp=sp, t=conv_rd):
                                QY.w(e, ac, ac.n)
                                QY.w(e, dv, t)
                                QY.w(e, hw, hbcd_t)
                                for loc in range(2):
                                    tk = (2 * sp + loc) * 128
                                    src = hbc_d.ap().rearrange("(ct p) T -> p ct T",
                                                               p=128)[:, :, tk:tk + 128]
                                    dst = hbc_sp.rearrange("p (ct l q) -> p ct l q",
                                                           ct=NCT, l=2)[:, :, loc:loc + 1,
                                                                        3:131].squeeze(2)
                                    QY.w(e, hw, hw.n)
                                    hw.inc(e.dma_start(out=dst, in_=src))
                            QY(_)
                            hbl_t = hw.n

                            for loc in range(2):
                                def _(e, sp=sp, loc=loc, hp=halo_place):
                                    QG.w(e, sw, sw.n)
                                    QG.w(e, cc, halo_cc)
                                    QG.w(e, dv, hp)
                                    hg128 = halo_gt.ap().rearrange("n (p q) -> (n p) q", p=128)
                                    sw.inc(e.indirect_dma_start(
                                        out=halo_sb[:, :], out_offset=None,
                                        in_=hg128,
                                        in_offset=IndirectOffsetOnAxis(
                                            hidx128[:, 2 * sp + loc:2 * sp + loc + 1], 0)))
                                QG(_)
                                hldi_t = sw.n

                                def _(e, sp=sp, loc=loc, t=hldi_t, t2=hbl_t):
                                    QV.w(e, sw, t)
                                    QV.w(e, hw, t2)
                                    dst = hbc_sp.rearrange("p (ct l q) -> p ct l q",
                                                           ct=NCT, l=2)[:, :, loc, 0:3]
                                    dv.inc(e.tensor_single_scalar(
                                        dst, halo_sb.ap().rearrange("p (ct k) -> p ct k", ct=NCT),
                                        hmask[:, 2 * sp + loc:2 * sp + loc + 1], ALU.mult))
                                QV(_)
                                halo_place = dv.n
                            halo_dv = dv.n

                            sil_ac = ac.n
                            for ct in range(NCT):
                                def _(e, sp=sp, ct=ct, t=halo_dv, ta=sil_ac):
                                    QV.w(e, dv, t)
                                    QV.w(e, ac, ta)
                                    for loc in range(2):
                                        base = ct * 262 + loc * 131
                                        for k in range(KT):
                                            shifted = hbc_sp[:, base + k:base + k + 128]
                                            wcol = convw[:, ct * KT + k:ct * KT + k + 1]
                                            if k == 0:
                                                dv.inc(e.tensor_single_scalar(
                                                    sc1[:, loc * 128:loc * 128 + 128],
                                                    shifted, wcol, ALU.mult))
                                            else:
                                                QV.w(e, dv, dv.n)
                                                dv.inc(e.scalar_tensor_tensor(
                                                    sc1[:, loc * 128:loc * 128 + 128],
                                                    shifted, wcol,
                                                    sc1[:, loc * 128:loc * 128 + 128],
                                                    ALU.mult, ALU.add))
                                QV(_)
                                tcv = dv.n

                                def _(e, sp=sp, ct=ct, tcv=tcv, xr=xr_rd):
                                    QS.w(e, dv, max(tcv, xr))
                                    if ct < 32:
                                        dst = xs_sp[:, ct * 256:(ct + 1) * 256]
                                    elif ct == 32:
                                        dst = ball[:, 0:256]
                                    else:
                                        dst = call[:, sp * 256:(sp + 1) * 256]
                                    ac.inc(e.activation(dst, sc1[:, 0:256], AF.Silu,
                                                        bias=convb[:, ct:ct + 1]))
                                QS(_)
                                sil_ac = ac.n
                            conv_ac = ac.n

                            for loc in range(2):
                                j = 2 * sp + loc
                                ccol = sp * 256 + loc * 128

                                def _(e, loc=loc, ccol=ccol, t=conv_ac,
                                      w1=tb_par[0], w2=at_par[0]):
                                    QP.w(e, ac, t)
                                    QP.w(e, dv, max(w1, w2))
                                    pe.inc(e.matmul(ps_cb[:, 0:128],
                                                    ball[:, loc * 128:loc * 128 + 128],
                                                    call[:, ccol:ccol + 128],
                                                    start=True, stop=True))
                                    pe.inc(e.transpose(ps_tb2[:, 0:128],
                                                       ball[:, loc * 128:loc * 128 + 128],
                                                       identbf[:, :]))
                                QP(_)
                                tcbb = pe.n

                                def _(e, tcbb=tcbb):
                                    QV.w(e, pe, tcbb)
                                    dv.inc(e.tensor_tensor(cbm[:, :], ps_cb[:, 0:128],
                                                           utmask[:, :], ALU.mult))
                                    dv.inc(e.tensor_copy(btm[:, :], ps_tb2[:, 0:128]))
                                QV(_)
                                tb_par[0] = dv.n
                                cb_dv = dv.n

                                trd = dict(tb_par)
                                for tt in range(32):
                                    par = tt % 2

                                    def _(e, tt=tt, par=par, loc=loc, cd=conv_ac,
                                          w=trd[tt % 2], xw=ymst_ev):
                                        QP.w(e, ac, cd)
                                        QP.w(e, dv, max(w, xw))
                                        pe.inc(e.transpose(
                                            ps_tb2[:, par * 1024:par * 1024 + 128],
                                            xs_sp[:, tt * 256 + loc * 128:tt * 256 + loc * 128 + 128],
                                            identbf[:, :]))
                                    QP(_)

                                    def _(e, tt=tt, par=par, t=pe.n):
                                        QV.w(e, pe, t)
                                        dv.inc(e.tensor_copy(
                                            xr_tm[:, tt * 128:(tt + 1) * 128],
                                            ps_tb2[:, par * 1024:par * 1024 + 128]))
                                    QV(_)
                                    trd[par] = dv.n
                                tb_par.update(trd)
                                xr_dv = dv.n

                                mmh = {}
                                acp = {0: ac.n, 1: ac.n}
                                for h in range(MH):
                                    parh = h % 2
                                    hh = h % 8

                                    def _(e, j=j, h=h, parh=parh, t=xr_dv, w=at_par[parh]):
                                        QP.w(e, dv, max(t, w))
                                        lhs = acum.ap()[:, j * MH + h:j * MH + h + 1] \
                                            .broadcast_to([128, 128])
                                        pe.inc(e.matmul(ps_at[:, parh * 512:parh * 512 + 128],
                                                        lhs, ident32[:, :],
                                                        start=True, stop=True))
                                    QP(_)
                                    tat = pe.n

                                    def _(e, j=j, h=h, parh=parh, tat=tat, wa=acp[parh]):
                                        QV.w(e, pe, tat)
                                        QV.w(e, ac, wa)
                                        dv.inc(e.tensor_scalar(
                                            dmt[parh][:, :], ps_at[:, parh * 512:parh * 512 + 128],
                                            nacum[:, j * MH + h:j * MH + h + 1],
                                            0.0, ALU.add, ALU.min))
                                    QV(_)
                                    at_par[parh] = dv.n
                                    tdm = dv.n

                                    def _(e, parh=parh, tdm=tdm):
                                        QS.w(e, dv, tdm)
                                        ac.inc(e.activation(pex[parh][:, :], dmt[parh][:, :],
                                                            AF.Exp))
                                    QS(_)
                                    tpx = ac.n
                                    acp[parh] = tpx

                                    def _(e, j=j, h=h, hh=hh, parh=parh, tpx=tpx,
                                          cb_dv=cb_dv, wm=mmh.get(h - 2, 0)):
                                        QV.w(e, ac, tpx)
                                        QV.w(e, dv, cb_dv)
                                        QV.w(e, pe, wm)
                                        dv.inc(e.tensor_tensor(ptb[parh][:, :], pex[parh][:, :],
                                                               cbm[:, :], ALU.mult))
                                        dcol = dtv.ap()[:, j * MH + h:j * MH + h + 1]
                                        ecol = dte.ap()[:, j * MH + h:j * MH + h + 1]
                                        dv.inc(e.tensor_single_scalar(
                                            hgslc[:, hh * 64:hh * 64 + 64],
                                            xr_tm[:, h * 64:(h + 1) * 64], dcol, ALU.mult))
                                        dv.inc(e.tensor_single_scalar(
                                            hgslc2[:, hh * 64:hh * 64 + 64],
                                            xr_tm[:, h * 64:(h + 1) * 64], ecol, ALU.mult))
                                    QV(_)
                                    tpt = dv.n

                                    def _(e, h=h, hh=hh, parh=parh, tpt=tpt, ev=ymst_ev):
                                        QP.w(e, dv, tpt)
                                        if hh == 0:
                                            QP.w(e, dv, ev)
                                        pe.inc(e.matmul(ps_ym[:, hh * 64:(hh + 1) * 64],
                                                        ptb[parh][:, :],
                                                        hgslc[:, hh * 64:(hh + 1) * 64],
                                                        start=(hh == 0), stop=(hh == 7)))
                                        pe.inc(e.matmul(ps_st[:, hh * 64:(hh + 1) * 64],
                                                        btm[:, :],
                                                        hgslc2[:, hh * 64:(hh + 1) * 64],
                                                        start=(hh == 0), stop=(hh == 7)))
                                    QP(_)
                                    mmh[h] = pe.n

                                    if hh == 7:
                                        hg0 = (h // 8) * 8

                                        def _(e, j=j, hg0=hg0, t=mmh[h], st=scon_st):
                                            QV.w(e, pe, t)
                                            QV.w(e, hw, st)
                                            dco = dbc.ap()[:, hg0:hg0 + 8].unsqueeze(2) \
                                                .broadcast_to([128, 8, 64])
                                            dv.inc(e.tensor_tensor(
                                                xrd.ap().rearrange("p (a b) -> p a b", a=8),
                                                xr_tm[:, hg0 * 64:hg0 * 64 + 512]
                                                .rearrange("p (a b) -> p a b", a=8),
                                                dco, ALU.mult))
                                            QV.w(e, dv, dv.n)
                                            dv.inc(e.tensor_tensor(
                                                uni[:, j * MI + hg0 * 64:j * MI + hg0 * 64 + 512],
                                                ps_ym[:, 0:512], xrd[:, 0:512], ALU.add))
                                            dv.inc(e.tensor_copy(
                                                scon[:, hg0 * 64:hg0 * 64 + 512],
                                                ps_st[:, 0:512]))
                                        QV(_)
                                        ymst_ev = dv.n

                                def _(e, j=j, t=ymst_ev, st=scon_st):
                                    QY.w(e, dv, t)
                                    QY.w(e, hw, st)
                                    dst = s_cbd.ap()[j * 128:(j + 1) * 128, :]
                                    QY.w(e, hw, hw.n)
                                    hw.inc(e.dma_start(out=dst, in_=scon[:, :]))
                                QY(_)
                                scon_st = hw.n
                                xr_rd = ymst_ev
                            conv_rd = dv.n
                        sexp_t = hw.n
                        if int(_os.environ.get('KMTR', '9')) < 2:
                            return

                        def _(e, t=sexp_t):
                            QG.w(e, hw, t)
                            QG.w(e, hw, tcb_t)
                            cc.inc(e.collective_compute(
                                "AllGather", ALU.bypass, replica_groups=RG,
                                ins=[s_cbd.ap()], outs=[s_gtd.ap()]))
                            cc.inc(e.collective_compute(
                                "AllGather", ALU.bypass, replica_groups=RG,
                                ins=[t_cbd.ap()], outs=[t_gtd.ap()]))
                        QG(_)
                        sg_cc = cc.n

                        def _(e, t=sg_cc):
                            QG.w(e, cc, t)
                            QG.w(e, sw, sw.n)
                            sw.inc(e.indirect_dma_start(
                                out=sc2[0:32, 0:MH], out_offset=None,
                                in_=t_gtd.ap(),
                                in_offset=IndirectOffsetOnAxis(sidx[:, 0:1], 0)))
                        QG(_)
                        t32_t = sw.n

                        def _(e, t=t32_t):
                            QY.w(e, sw, t)
                            QY.w(e, hw, hw.n)
                            hw.inc(e.dma_start(out=t32_d.ap(), in_=sc2[0:32, 0:MH]))
                            e.wait_ge(hw.h, hw.n)
                            QY.w(e, hw, hw.n)
                            hw.inc(e.dma_start(out=t32r[0:1, :],
                                               in_=t32_d.ap().rearrange("a b -> (a b)").unsqueeze(0)))
                        QY(_)
                        t32r_t = hw.n

                    S_run = REG[:, 0:4096]
                    S_run_bf = REG[:, 4096:6144].bitcast(BF16)
                    sfor = [REG[:, 6144:8192].bitcast(BF16), REG[:, 8192:10240].bitcast(BF16)]
                    etfm = REG[:, 10240:12288]

                    MC = ExitStack()
                    with MC:
                        ps_e = MC.enter_context(nc.psum_tensor("ps_e_m", [128, 512], F32))
                        ps_y2 = MC.enter_context(nc.psum_tensor("ps_y2_m", [128, 512], F32))
                        ps_tbc = MC.enter_context(nc.psum_tensor("ps_tbc_m", [128, 2048], BF16))

                        def _(e, t=t32r_t):
                            QS.w(e, hw, t)
                            ac.inc(e.activation(t32r[0:1, :], t32r[0:1, :], AF.Exp))
                        QS(_)
                        e32_t = ac.n

                        etf_dv = dv.n
                        for q4 in range(4):
                            def _(e, q4=q4, t=e32_t, ev=etf_dv):
                                QP.w(e, ac, t)
                                QP.w(e, dv, ev)
                                pe.inc(e.matmul(ps_e[:, 0:512], onesrow[:, :],
                                                t32r[0:1, q4 * 512:(q4 + 1) * 512],
                                                start=True, stop=True))
                            QP(_)

                            def _(e, q4=q4, t=pe.n):
                                QV.w(e, pe, t)
                                dv.inc(e.tensor_copy(etfm[:, q4 * 512:(q4 + 1) * 512],
                                                     ps_e[:, 0:512]))
                            QV(_)
                            etf_dv = dv.n

                        def _(e, t=etf_dv):
                            QV.w(e, dv, t)
                            wmb = wmask.ap().unsqueeze(2).broadcast_to([128, 32, MH])
                            iwb = iwmask.ap().unsqueeze(2).broadcast_to([128, 32, MH])
                            ev = etfm.rearrange("p (s h) -> p s h", s=32)
                            dv.inc(e.tensor_tensor(ev, ev, wmb, ALU.mult))
                            QV.w(e, dv, dv.n)
                            dv.inc(e.tensor_tensor(ev, ev, iwb, ALU.add))
                            QV.w(e, dv, dv.n)
                            dv.inc(e.memset(S_run, 0.0))
                        QV(_)
                        srun_dv = dv.n

                        y2_ev = dv.n
                        sld = {}
                        for s in range(32):
                            par = s % 2

                            def _(e, s=s, par=par):
                                QG.w(e, sw, sw.n)
                                QG.w(e, cc, sg_cc)
                                if s >= 2:
                                    QG.w(e, dv, sld[s - 2])
                                sw.inc(e.indirect_dma_start(
                                    out=sfor[par], out_offset=None,
                                    in_=s_gtd.ap(),
                                    in_offset=IndirectOffsetOnAxis(
                                        sidx128[:, s:s + 1], 0)))
                            QG(_)
                            sfl = sw.n

                            def _(e, s=s, par=par, sfl=sfl, t=srun_dv):
                                QV.w(e, dv, t)
                                QV.w(e, sw, sfl)
                                eb = etfm.rearrange("p (a h) -> p a h", a=32)[
                                    :, s:s + 1, :].unsqueeze(3).broadcast_to([128, 1, MH, 64])
                                sv = S_run.rearrange("p (h q) -> p 1 h q" if False
                                                     else "p (h q) -> p h q", h=MH) \
                                    .unsqueeze(1) if False else None
                                svv = S_run.rearrange("p (h q) -> p h q", h=MH).unsqueeze(1)
                                dv.inc(e.tensor_tensor(svv, svv, eb, ALU.mult))
                                QV.w(e, dv, dv.n)
                                dv.inc(e.scalar_tensor_tensor(
                                    S_run, sfor[par], wmask[:, s:s + 1], S_run,
                                    ALU.mult, ALU.add))
                            QV(_)
                            srun_dv = dv.n
                            sld[s] = dv.n

                            if s % 4 == 2:
                                j = s // 4

                                def _(e, t=srun_dv):
                                    QV.w(e, dv, t)
                                    dv.inc(e.tensor_copy(S_run_bf, S_run))
                                QV(_)
                                cast_t = dv.n

                                for hg in range(8):
                                    def _(e, j=j, hg=hg, t=cast_t, ev=y2_ev):
                                        QP.w(e, dv, max(t, ev))
                                        mm = None
                                        for hh in range(8):
                                            h = hg * 8 + hh
                                            mm = e.matmul(
                                                ps_y2[:, hh * 64:(hh + 1) * 64],
                                                call[:, j * 128:(j + 1) * 128],
                                                S_run_bf[:, h * 64:(h + 1) * 64],
                                                start=(hh == 0), stop=(hh == 7))
                                        pe.inc(mm)
                                    QP(_)
                                    ty2 = pe.n

                                    def _(e, j=j, hg=hg, ty2=ty2):
                                        QV.w(e, pe, ty2)
                                        ebc = eav.ap()[:, j * MH + hg * 8:j * MH + hg * 8 + 8] \
                                            .unsqueeze(2).broadcast_to([128, 8, 64])
                                        dv.inc(e.tensor_tensor(
                                            sc1.ap().rearrange("p (a b) -> p a b", a=8),
                                            ps_y2.ap().rearrange("p (a b) -> p a b", a=8),
                                            ebc, ALU.mult))
                                        QV.w(e, dv, dv.n)
                                        dst = uni[:, j * MI + hg * 512:j * MI + hg * 512 + 512]
                                        dv.inc(e.tensor_tensor(dst, dst, sc1[:, 0:512], ALU.add))
                                    QV(_)
                                    y2_ev = dv.n

                        gld = {}
                        gmm = {}
                        gate_ev = y2_ev
                        for og in range(16):
                            buf = wgb[og % 2]

                            def _(e, og=og, buf=buf, wd=srun_dv):
                                QG.w(e, sw, sw.n)
                                QG.w(e, dv, wd)
                                if og >= 2:
                                    QG.w(e, pe, gmm[og - 2])
                                src = w_inT.ap()[:, 256 * og:256 * og + 256].rearrange(
                                    "(dc p) o -> p dc o", p=128)
                                dstv = buf.rearrange("p (dc o) -> p dc o", dc=16)
                                sw.inc(e.dma_start(out=dstv, in_=src))
                            QG(_)
                            gld[og] = sw.n

                            for j in range(NT):
                                def _(e, og=og, j=j, buf=buf, t=gld[og], ta=ac.n):
                                    QP.w(e, sw, t)
                                    QP.w(e, ac, ta)
                                    mm = None
                                    for dc in range(16):
                                        mm = e.matmul(
                                            ps_e[:, 0:256],
                                            xhat[:, dc * TOK + j * 128:dc * TOK + j * 128 + 128],
                                            buf[:, dc * 256:(dc + 1) * 256],
                                            start=(dc == 0), stop=(dc == 15))
                                    pe.inc(mm)
                                QP(_)
                                gmm[og] = pe.n

                                def _(e, t=pe.n, gd=gate_ev):
                                    QS.w(e, pe, t)
                                    QS.w(e, dv, gd)
                                    ac.inc(e.activation(gsil[:, 0:256], ps_e[:, 0:256], AF.Silu))
                                QS(_)
                                tgs = ac.n

                                def _(e, og=og, j=j, tgs=tgs, yv=y2_ev):
                                    QV.w(e, ac, tgs)
                                    QV.w(e, dv, yv)
                                    dst = uni[:, j * MI + og * 256:j * MI + og * 256 + 256]
                                    dv.inc(e.tensor_tensor(dst, dst, gsil[:, 0:256], ALU.mult))
                                QV(_)
                                gate_ev = dv.n
                        gate_dv = gate_ev

                        ytmp = xhat.ap()[:, 0:4096]
                        yfm = xhat.ap()[:, 4096:8192]
                        sqdump = REG[:, 6144:10240]

                        yst_hw = hw.n
                        tr_ev = {0: dv.n, 1: dv.n}
                        rms_dv = dv.n
                        prev_yt = dv.n
                        for j in range(NT):
                            def _(e, j=j, t=gate_dv, pv=prev_yt):
                                QS.w(e, dv, max(t, pv))
                                QS.w(e, ac, ac.n)
                                ac.inc(e.activation(sqdump, uni[:, j * MI:(j + 1) * MI],
                                                    AF.Square, accum_out=rstat[:, 0:1]))
                                QS.w(e, ac, ac.n)
                                ac.inc(e.activation(rstat[:, 1:2], rstat[:, 0:1], AF.Sqrt,
                                                    scale=np.float32(1.0 / MI),
                                                    bias=np.float32(EPS)))
                            QS(_)
                            trms = ac.n

                            def _(e, j=j, trms=trms, st=yst_hw):
                                QV.w(e, ac, trms)
                                QV.w(e, hw, st)
                                dv.inc(e.reciprocal(rstat[:, 2:3], rstat[:, 1:2]))
                                QV.w(e, dv, dv.n)
                                dv.inc(e.tensor_single_scalar(
                                    ytmp, uni[:, j * MI:(j + 1) * MI],
                                    rstat[:, 2:3], ALU.mult))
                            QV(_)
                            yt_t = dv.n
                            prev_yt = dv.n

                            for tt in range(32):
                                par = tt % 2

                                def _(e, tt=tt, par=par, yt_t=yt_t, w=tr_ev[par]):
                                    QP.w(e, dv, max(yt_t, w))
                                    pe.inc(e.transpose(
                                        ps_tbc[:, par * 1024:par * 1024 + 128],
                                        ytmp[:, tt * 128:(tt + 1) * 128], identbf[:, :]))
                                QP(_)

                                def _(e, tt=tt, par=par, t=pe.n):
                                    QV.w(e, pe, t)
                                    dv.inc(e.tensor_copy(yfm[:, tt * 128:(tt + 1) * 128],
                                                         ps_tbc[:, par * 1024:par * 1024 + 128]))
                                QV(_)
                                tr_ev[par] = dv.n
                            yfm_t = dv.n

                            def _(e, j=j, t=yfm_t):
                                QY.w(e, dv, t)
                                QY.w(e, hw, hw.n)
                                hw.inc(e.dma_start(out=yfm_d.ap()[j * 128:(j + 1) * 128, :],
                                                   in_=yfm))
                            QY(_)
                            yst_hw = hw.n
                        yall_hw = hw.n

                        wobf_all = xhat.ap()[:, 0:16384]
                        wof32 = REG[:, 2048:6144]
                        ybuf = REG[:, 6144:8192].bitcast(BF16)

                        womm = pe.n
                        oproj_ev = dv.n
                        ybl_t = hw.n
                        wc_prev = dv.n
                        for ob in range(4):
                            wc_t = dv.n
                            for ig in range(4):
                                def _(e, ob=ob, ig=ig, t=yall_hw, wp=wc_prev):
                                    QY.w(e, hw, hw.n)
                                    QY.w(e, dv, wp)
                                    if ob == 0 and ig == 0:
                                        QY.w(e, dv, gate_dv)
                                    src = w_outT.ap()[ig * 1024:(ig + 1) * 1024,
                                                      ob * 512:(ob + 1) * 512] \
                                        .rearrange("(ic p) o -> p ic o", p=128)
                                    dst = wof32.rearrange("p (ic o) -> p ic o", ic=8)
                                    QY.w(e, hw, hw.n)
                                    hw.inc(e.dma_start(out=dst, in_=src))
                                QY(_)
                                wl_t = hw.n

                                def _(e, ob=ob, ig=ig, t=wl_t, tw=womm):
                                    QV.w(e, hw, t)
                                    QV.w(e, pe, tw)
                                    if ob == 0 and ig == 0:
                                        QV.w(e, dv, max(tr_ev[0], tr_ev[1]))
                                    for ic in range(8):
                                        dv.inc(e.tensor_single_scalar(
                                            wobf_all[:, (ig * 8 + ic) * 512:(ig * 8 + ic + 1) * 512],
                                            wof32[:, ic * 512:(ic + 1) * 512],
                                            normw[:, ig * 8 + ic:ig * 8 + ic + 1],
                                            ALU.mult))
                                QV(_)
                                wc_t = dv.n
                                wc_prev = dv.n

                            for j in range(NT):
                                def _(e, j=j, ob=ob, t=oproj_ev):
                                    QY.w(e, dv, t)
                                    QY.w(e, hw, yall_hw)
                                    QY.w(e, hw, hw.n)
                                    hw.inc(e.dma_start(out=ybuf,
                                                       in_=yfm_d.ap()[j * 128:(j + 1) * 128, :]))
                                QY(_)
                                ybl_t = hw.n

                                def _(e, j=j, ob=ob, t=ybl_t, tw=wc_t, ev=oproj_ev):
                                    QP.w(e, hw, t)
                                    QP.w(e, dv, max(tw, ev))
                                    mm = None
                                    for ic in range(32):
                                        mm = e.matmul(
                                            ps_y2[:, 0:512],
                                            ybuf[:, ic * 128:(ic + 1) * 128],
                                            wobf_all[:, ic * 512:(ic + 1) * 512],
                                            start=(ic == 0), stop=(ic == 31))
                                    pe.inc(mm)
                                QP(_)
                                womm = pe.n

                                def _(e, j=j, ob=ob, t=womm):
                                    QV.w(e, pe, t)
                                    dv.inc(e.tensor_copy(
                                        uni[:, j * MI + ob * 512:j * MI + ob * 512 + 512],
                                        ps_y2[:, 0:512]))
                                QV(_)
                                oproj_ev = dv.n
                        oall = dv.n

                        def _(e, t=oall):
                            QY.w(e, dv, t)
                            xr = x_spill.ap().rearrange("(n p) d -> p n d", p=128)
                            xdst = x_sb.ap().rearrange("p (n d) -> p n d", n=NT)
                            QY.w(e, hw, hw.n)
                            hw.inc(e.dma_start(out=xdst, in_=xr))
                        QY(_)
                        xrl_t = hw.n

                        def _(e, t=xrl_t):
                            QV.w(e, hw, t)
                            for j in range(NT):
                                dst = x_sb[:, j * D:(j + 1) * D]
                                dv.inc(e.tensor_tensor(dst, dst,
                                                       uni[:, j * MI:j * MI + D], ALU.add))
                        QV(_)
                    return

            if TRUNC >= 1:
                emit_ffn("f1", {dv: consts_t})

            if STAGES >= 2:
                emit_mamba({dv: dv.n, pe: pe.n, ac: ac.n, po: po.n})

            if STAGES >= 3:
                emit_ffn("f3", {dv: dv.n, pe: pe.n, ac: ac.n, po: po.n, hw: hw.n, sw: sw.n})

            def _(e):
                QY.w(e, dv, dv.n)
                QY.w(e, pe, pe.n)
                QY.w(e, ac, ac.n)
                QY.w(e, po, po.n)
                xo = xout.ap().rearrange("(n p) d -> p n d", p=128)
                xsrc = x_sb.ap().rearrange("p (n d) -> p n d", n=NT)
                QY.w(e, hw, hw.n)
                hw.inc(e.dma_start(out=xo, in_=xsrc))
                e.wait_ge(hw.h, hw.n)
            QY(_)

    return nc


def _host_consts(c):
    out = {}
    ii = np.arange(128)
    out["ident32"] = np.eye(128, dtype=np.float32)
    out["identbf"] = np.eye(128, dtype=np.float32)
    out["utones"] = (ii[:, None] <= ii[None, :]).astype(np.float32)
    out["utmaskbf"] = (ii[None, :] >= ii[:, None]).astype(np.float32)
    out["onesrow"] = np.ones((1, 128), np.float32)
    am = np.zeros((128, 32 * 128), np.float32)
    zb = np.zeros((128, 32), np.float32)
    for j in range(NT):
        for i in range(4):
            m = 4 * j + i
            g = 4 * j + c
            col = 4 * j + i
            if m < g:
                am[:, col * 128:(col + 1) * 128] = 1.0
                zb[:, col] = EXP_OFF
            elif m == g:
                am[:, col * 128:(col + 1) * 128] = (ii[:, None] <= ii[None, :])
                zb[:, col] = EXP_OFF
            else:
                zb[:, col] = NEG_BIG
    out["attmask"] = am
    out["zbias"] = zb
    sidx = np.zeros((32, 1), np.int32)
    sidx128 = np.zeros((128, 32), np.int32)
    wm = np.zeros((128, 32), np.float32)
    pp = np.arange(128, dtype=np.int32)
    for j in range(NT):
        for r in range(4):
            st = 4 * j + r
            m = (4 * j + c - 3 + r) if r < 3 else (4 * j + c)
            if 0 <= m < 32:
                slot = (m % 4) * 8 + (m // 4)
                sidx[st, 0] = slot
                sidx128[:, st] = slot * 128 + pp
                wm[:, st] = 1.0
    out["sidx"] = sidx
    out["sidx128"] = sidx128
    out["wmask"] = wm
    out["iwmask"] = 1.0 - wm
    hidx128 = np.zeros((128, NT), np.int32)
    hm = np.zeros((128, NT), np.float32)
    for j in range(NT):
        m = 4 * j + c - 1
        if m >= 0:
            slot = (m % 4) * 8 + (m // 4)
            hidx128[:, j] = slot * 128 + pp
            hm[:, j] = 1.0
    out["hidx128"] = hidx128
    out["hmask"] = hm
    return out


_CACHE = {}


def kernel(L0_attn_x, L0_attn_W_in, L0_attn_conv_w, L0_attn_conv_b, L0_attn_dt_bias,
           L0_attn_A_log, L0_attn_D, L0_attn_norm_w, L0_attn_W_out,
           L0_ffn_W_gate, L0_ffn_W_qkv):
    import ml_dtypes
    if "nc" not in _CACHE:
        _CACHE["nc"] = build_nc()
    nc = _CACHE["nc"]

    x = np.ascontiguousarray(np.asarray(L0_attn_x, dtype=np.float32))
    shared = {
        "w_qkvT": np.ascontiguousarray(np.asarray(L0_ffn_W_qkv, np.float32).T),
        "w_gT": np.ascontiguousarray(np.asarray(L0_ffn_W_gate, np.float32).T),
        "w_inT": np.ascontiguousarray(np.asarray(L0_attn_W_in, np.float32).T),
        "w_outT": np.ascontiguousarray(np.asarray(L0_attn_W_out, np.float32).T),
        "convw_r": np.ascontiguousarray(
            np.asarray(L0_attn_conv_w, np.float32).reshape(NCT, 128, KT)
            .transpose(1, 0, 2).reshape(128, NCT * KT)),
        "convb_r": np.ascontiguousarray(
            np.asarray(L0_attn_conv_b, np.float32).reshape(NCT, 128).T),
        "dtb_row": np.asarray(L0_attn_dt_bias, np.float32).reshape(1, MH),
        "alog_row": np.asarray(L0_attn_A_log, np.float32).reshape(1, MH),
        "dvec_row": np.asarray(L0_attn_D, np.float32).reshape(1, MH),
        "normw_r": np.ascontiguousarray(
            np.asarray(L0_attn_norm_w, np.float32).reshape(MI // 128, 128).T),
    }
    in_maps = []
    for core in range(8):
        b, c = core // 4, core % 4
        consts = _host_consts(c)
        for k in ("identbf", "utmaskbf", "attmask"):
            consts[k] = consts[k].astype(ml_dtypes.bfloat16)
        xs = np.concatenate([x[b, 512 * j + 128 * c:512 * j + 128 * (c + 1), :]
                             for j in range(NT)], axis=0)
        m = {"xin": np.ascontiguousarray(xs)}
        m.update(shared)
        m.update(consts)
        in_maps.append(m)

    res = bass_utils.run_bass_kernel_spmd(nc, in_maps, core_ids=list(range(8)))
    out = np.zeros((B, T, D), np.float32)
    for core in range(8):
        b, c = core // 4, core % 4
        xo = res.results[core]["xout"]
        for j in range(NT):
            out[b, 512 * j + 128 * c:512 * j + 128 * (c + 1), :] = \
                xo[j * 128:(j + 1) * 128, :]
    return out
